# version 01
# baseline (speedup 1.0000x reference)
import numpy as np
import ml_dtypes
from contextlib import ExitStack

import concourse.bass as bass
import concourse.mybir as mybir
from concourse.bass_utils import run_bass_kernel_spmd

F32 = mybir.dt.float32
BF16 = mybir.dt.bfloat16
I32 = mybir.dt.int32
AF = mybir.ActivationFunctionType
ALU = mybir.AluOpType
AX = mybir.AxisListType
BF = ml_dtypes.bfloat16

D = 2048
T = 1024
S = 2048
HEADS = 16
HD = 128
HKV = 4
KVD = 512
RANK = 16
H0 = 5120
MLP = 16384
NQKV = 3 * D
LN_EPS = 1e-5
RMS_EPS = float(np.finfo(np.float32).eps)
NEG = -1e30
R126 = float(2.0 ** 126)
ISCALE = 0.08838834764831845


def _parity_perm(n):
    out = np.empty(n, np.int64)
    for b in range(n // 256):
        out[b * 256:b * 256 + 128] = b * 256 + 2 * np.arange(128)
        out[b * 256 + 128:(b + 1) * 256] = b * 256 + 2 * np.arange(128) + 1
    return out


PI = _parity_perm(D)
PI_INV = np.argsort(PI)
PI_N = _parity_perm(MLP)


def tile_lhsT(w):
    K, N = w.shape
    return np.ascontiguousarray(
        w.reshape(K // 128, 128, N // 128, 128).transpose(2, 1, 0, 3))


class SemCtr:
    def __init__(self, h):
        self.h = h
        self.n = 0

    def inc(self, inst, k=16):
        inst.then_inc(self.h, k)
        self.n += k

    def wait(self, eng, n=None):
        eng.wait_ge(self.h, self.n if n is None else n)


_DRAIN_OPS = {
    "tensor_tensor", "tensor_scalar", "scalar_tensor_tensor", "tensor_copy",
    "tensor_reduce", "reduce_sum", "reduce_max", "memset", "memzero",
    "activation", "sign", "copy", "mul", "add", "square", "sqrt",
    "reciprocal", "select", "tensor_tensor_scan", "iota", "bn_stats",
    "bn_aggr", "tensor_add", "tensor_sub", "tensor_mul", "tensor_max",
    "tensor_scalar_add", "tensor_scalar_sub", "tensor_scalar_mul",
    "tensor_single_scalar", "tensor_relu",
}


class W:

    def __init__(self, eng):
        self._e = eng

    def __getattr__(self, k):
        f = getattr(self._e, k)
        if k in _DRAIN_OPS:
            def g(*a, **kw):
                inst = f(*a, **kw)
                self._e.drain()
                return inst
            return g
        return f


class Prog:
    def __init__(self):
        self.nc = bass.Bass("TRN2", target_bir_lowering=False)
        self.ctx = ExitStack()
        self._id = 0

    def sem(self):
        self._id += 1
        return SemCtr(self.ctx.enter_context(self.nc.semaphore(f"s{self._id}")))

    def sbuf(self, name, shape, dtype):
        return self.ctx.enter_context(self.nc.sbuf_tensor(name, shape, dtype))

    def psum(self, name, shape, dtype=F32):
        return self.ctx.enter_context(self.nc.psum_tensor(name, shape, dtype))

    def begin(self):
        self.block = self.ctx.enter_context(self.nc.Block())
        return self.block

    def done(self):
        self.ctx.close()
        return self.nc



def build_preprocess(n_cast_rows, phases=(1, 2, 3, 4)):
    p = Prog()
    nc = p.nc
    dt = nc.dram_tensor

    wfc_full = dt("wfc_full", [H0, D], F32, kind="ExternalInput").ap()
    wpj_full = dt("wpj_full", [D, H0], F32, kind="ExternalInput").ap()
    wfc_sl = dt("wfc_sl", [H0, 256], F32, kind="ExternalInput").ap()
    wpj_sl = dt("wpj_sl", [D, 640], F32, kind="ExternalInput").ap()
    cast_in = dt("cast_in", [n_cast_rows * 128, NQKV], F32, kind="ExternalInput").ap()
    g_pk = dt("g_pk", [128, MLP], I32, kind="ExternalInput").ap()
    u_pk = dt("u_pk", [128, MLP], I32, kind="ExternalInput").ap()
    d_pk = dt("d_pk", [1024, D], I32, kind="ExternalInput").ap()
    g_sc = dt("g_sc", [128, MLP], F32, kind="ExternalInput").ap()
    u_sc = dt("u_sc", [128, MLP], F32, kind="ExternalInput").ap()
    d_sc = dt("d_sc", [1024, D], F32, kind="ExternalInput").ap()

    wfct_o = dt("wfct_o", [256, H0], BF16, kind="ExternalOutput").ap()
    wpjt_o = dt("wpjt_o", [640, D], BF16, kind="ExternalOutput").ap()
    cast_o = dt("cast_o", [n_cast_rows * 128, NQKV], BF16, kind="ExternalOutput").ap()
    g_o = dt("g_o", [256, MLP], BF16, kind="ExternalOutput").ap()
    u_o = dt("u_o", [256, MLP], BF16, kind="ExternalOutput").ap()
    do_o = dt("do_o", [2048, D], BF16, kind="ExternalOutput").ap()

    scr = dt("scr", [1, 8], F32, kind="Internal").ap()
    dbg_o = dt("dbg_o", [128, 8], F32, kind="ExternalOutput").ap()

    A = p.sbuf("A", [128, 16384], F32)[:]
    Bi = p.sbuf("Bi", [128, 16384], I32)[:]
    C = p.sbuf("C", [128, 16384], BF16)[:]
    sb_red = p.sbuf("sb_red", [128, 2], F32)
    sb_tmp1 = p.sbuf("sb_tmp1", [128, 1], F32)
    sb_ones = p.sbuf("sb_ones", [128, 1], F32)
    sb_am = p.sbuf("sb_am", [2, 1], F32)
    sb_nthr = p.sbuf("sb_nthr", [2, 1], F32)
    sb_bc = p.sbuf("sb_bc", [128, 4], F32)

    ps = p.psum("ps", [128, 512], F32)

    ld = p.sem()
    vc = p.sem()
    pe = p.sem()
    smv = p.sem()
    bcast = p.sem()
    ldt = p.sem()
    sA = p.sem()
    vct = p.sem()
    trd = p.sem()
    ldc = p.sem()
    vcc = p.sem()
    ldq = p.sem()
    sAb = p.sem()
    vq = p.sem()
    st = p.sem()
    block = p.begin()

    n_fc, n_pj = H0 // 128, D // 128
    NT = n_fc + n_pj

    wa, wb = A[:, 0:2048], A[:, 2048:4096]
    if 1 in phases:
        wfc_flat = wfc_full.rearrange("a b -> (a b)").rearrange("(t p m) -> t p m", p=128, m=2048)
        wpj_flat = wpj_full.rearrange("a b -> (a b)").rearrange("(t p m) -> t p m", p=128, m=2048)
        NT1 = 80

        @block.sync
        def _(sync):
            for i in range(NT1):
                src = wfc_flat[i] if i < 40 else wpj_flat[i - 40]
                if i >= 2:
                    sAb.wait(sync, i - 1)
                ld.inc(sync.dma_start(out=(wa if i % 2 == 0 else wb), in_=src))

        absbuf = A[:, 4096:6144]

        @block.scalar
        def _(s):
            s = W(s)
            for i in range(NT1):
                ld.wait(s, 16 * (i + 1))
                if i >= 1:
                    vc.wait(s, i)
                sAb.inc(s.activation(absbuf, (wa if i % 2 == 0 else wb), AF.Abs), 1)

        @block.vector
        def _(v):
            v = W(v)
            v.memset(sb_ones[:], 1.0)
            v.memset(sb_red[:], 0.0)
            for i in range(NT1):
                sAb.wait(v, i + 1)
                col = 0 if i < 40 else 1
                v.tensor_reduce(sb_tmp1[:], absbuf, axis=AX.X, op=ALU.add)
                v.drain()
                vc.inc(v.tensor_tensor(sb_red[:, col:col + 1], sb_red[:, col:col + 1],
                                       sb_tmp1[:], ALU.add), 1)

        @block.tensor
        def _(t):
            vc.wait(t, NT1)
            pe.inc(t.matmul(ps[0:2, 0:1], sb_red[:], sb_ones[:], start=True, stop=True), 1)

        @block.vector
        def _(v):
            v = W(v)
            pe.wait(v, 1)
            cnt = float(H0 * D)
            v.tensor_scalar(sb_am[:], ps[0:2, 0:1], 1.0 / cnt, None, op0=ALU.mult)
            smv.inc(v.tensor_scalar(sb_nthr[:], sb_am[:], -0.7, None, op0=ALU.mult), 1)

        @block.sync
        def _(sync):
            smv.wait(sync, 1)
            st.inc(sync.dma_start(out=scr[0:1, 0:2], in_=sb_nthr[:]))
            st.inc(sync.dma_start(out=scr[0:1, 2:4], in_=sb_am[:]))
            st.wait(sync, 32)
            bcast.inc(sync.dma_start(out=sb_bc[:], in_=scr[0:1, 0:4].to_broadcast([128, 4])))

    ta = A[:, 0:640]
    s_, aa, mm = A[:, 1280:1920], A[:, 1920:2560], A[:, 2560:3200]
    tern = C[:, 0:640]
    trp = [C[:, 1024 + j * 128:1024 + (j + 1) * 128] for j in range(7)]
    widths = [256] * n_fc + [640] * n_pj
    ntr = [2] * n_fc + [5] * n_pj
    st_p2_base = st.n
    if 2 in phases:
        @block.sync
        def _(sync):
            bcast.wait(sync, 16)
            for i in range(NT):
                w = widths[i]
                src = (wfc_sl[i * 128:(i + 1) * 128, :] if i < n_fc
                       else wpj_sl[(i - n_fc) * 128:(i - n_fc + 1) * 128, :])
                if i >= 1:
                    sA.wait(sync, i)
                ldt.inc(sync.dma_start(out=ta[:, 0:w], in_=src))

        @block.scalar
        def _(s):
            s = W(s)
            for i in range(NT):
                w = widths[i]
                col = 0 if i < n_fc else 1
                ldt.wait(s, 16 * (i + 1))
                if i >= 1:
                    vct.wait(s, i)
                s.sign(s_[:, 0:w], ta[:, 0:w])
                s.activation(aa[:, 0:w], ta[:, 0:w], AF.Abs)
                s.sign(mm[:, 0:w], aa[:, 0:w], bias=sb_bc[:, col:col + 1])
                sA.inc(s.activation(mm[:, 0:w], mm[:, 0:w], AF.Relu), 1)
                vct.wait(s, i + 1)
                if i >= 1:
                    s.wait_ge(st.h, st_p2_base + 16 * int(np.sum(ntr[:i])))
                for j in range(ntr[i]):
                    trd.inc(s.dma_start_transpose(trp[j], tern[:, j * 128:(j + 1) * 128]))

        @block.vector
        def _(v):
            v = W(v)
            for i in range(NT):
                w = widths[i]
                col = 0 if i < n_fc else 1
                sA.wait(v, i + 1)
                if i >= 1:
                    trd.wait(v, int(np.sum([16 * k for k in ntr[:i]])))
                vct.inc(v.scalar_tensor_tensor(
                    tern[:, 0:w], mm[:, 0:w], sb_bc[:, 2 + col:3 + col],
                    s_[:, 0:w], op0=ALU.mult, op1=ALU.mult), 1)

        @block.gpsimd
        def _(g):
            trd_cum = np.cumsum([0] + [16 * k for k in ntr])
            for i in range(NT):
                trd.wait(g, int(trd_cum[i + 1]))
                for j in range(ntr[i]):
                    if i < n_fc:
                        out = wfct_o[j * 128:(j + 1) * 128, i * 128:(i + 1) * 128]
                    else:
                        out = wpjt_o[j * 128:(j + 1) * 128,
                                     (i - n_fc) * 128:(i - n_fc + 1) * 128]
                    st.inc(g.dma_start(out=out, in_=trp[j]))

    cst = A[:, 0:NQKV]
    cstb = C[:, 0:NQKV]
    st_p3_base = st.n
    if 3 in phases:

        @block.sync
        def _(sync):
            sA.wait(sync)
            vct.wait(sync)
            for i in range(n_cast_rows):
                if i >= 1:
                    vcc.wait(sync, i)
                ldc.inc(sync.dma_start(out=cst, in_=cast_in[i * 128:(i + 1) * 128, :]))
                vcc.wait(sync, i + 1)
                st.inc(sync.dma_start(out=cast_o[i * 128:(i + 1) * 128, :], in_=cstb))

        @block.vector
        def _(v):
            v = W(v)
            st.wait(v, st_p3_base)
            for i in range(n_cast_rows):
                ldc.wait(v, 16 * (i + 1))
                if i >= 1:
                    v.wait_ge(st.h, st_p3_base + 16 * i)
                vcc.inc(v.tensor_copy(cstb, cst), 1)

    W4 = 4096
    pk = Bi[:, 0:W4]
    t1 = Bi[:, 4096:8192]
    t2 = Bi[:, 8192:12288]
    bits = Bi[:, 12288:16384]
    scp = A[:, 0:W4]
    deqA = C[:, 0:W4]
    deqB = C[:, 4096:8192]

    chunk_seq = []
    for (pk_src, sc_src, out, tiles, w) in [
            (g_pk, g_sc, g_o, 1, MLP), (u_pk, u_sc, u_o, 1, MLP),
            (d_pk, d_sc, do_o, 8, D)]:
        for i in range(tiles):
            for cc in range(w // W4 if w >= W4 else 1):
                cw = min(W4, w)
                chunk_seq.append((pk_src, sc_src, out, i, cc, cw))
    NQ = len(chunk_seq)
    st_p4_base = st.n
    if 4 in phases:

        @block.sync
        def _(sync):
            vcc.wait(sync)

            for k, (pk_src, sc_src, out, i, cc, cw) in enumerate(chunk_seq):
                if k >= 1:
                    vq.wait(sync, 2 * k)
                ldq.inc(sync.dma_start(out=pk[:, 0:cw],
                                       in_=pk_src[i * 128:(i + 1) * 128, cc * W4:cc * W4 + cw]))
                ldq.inc(sync.dma_start(out=scp[:, 0:cw],
                                       in_=sc_src[i * 128:(i + 1) * 128, cc * W4:cc * W4 + cw]))
                vq.wait(sync, 2 * k + 1)
                st.inc(sync.dma_start(out=out[(2 * i) * 128:(2 * i + 1) * 128,
                                              cc * W4:cc * W4 + cw], in_=deqA[:, 0:cw]))
                vq.wait(sync, 2 * k + 2)
                st.inc(sync.dma_start(out=out[(2 * i + 1) * 128:(2 * i + 2) * 128,
                                              cc * W4:cc * W4 + cw], in_=deqB[:, 0:cw]))

        @block.vector
        def _(v):
            v = W(v)
            st.wait(v, st_p4_base)
            for k, (pk_src, sc_src, out, i, cc, cw) in enumerate(chunk_seq):
                ldq.wait(v, 32 * (k + 1))
                if k >= 1:
                    v.wait_ge(st.h, st_p4_base + 32 * (k - 1) + 16)
                v.tensor_scalar(t1[:, 0:cw], pk[:, 0:cw], 7, 22,
                                op0=ALU.bitwise_and, op1=ALU.logical_shift_left)
                v.tensor_scalar(t2[:, 0:cw], pk[:, 0:cw], 8, 28,
                                op0=ALU.bitwise_and, op1=ALU.logical_shift_left)
                v.tensor_tensor(bits[:, 0:cw], t1[:, 0:cw], t2[:, 0:cw], ALU.add)
                vq.inc(v.scalar_tensor_tensor(deqA[:, 0:cw], bits[:, 0:cw].bitcast(F32),
                                              R126, scp[:, 0:cw],
                                              op0=ALU.mult, op1=ALU.mult), 1)
                if k >= 1:
                    v.wait_ge(st.h, st_p4_base + 32 * (k - 1) + 32)
                v.tensor_scalar(t1[:, 0:cw], pk[:, 0:cw], 0x70, 18,
                                op0=ALU.bitwise_and, op1=ALU.logical_shift_left)
                v.tensor_scalar(t2[:, 0:cw], pk[:, 0:cw], 0x80, 24,
                                op0=ALU.bitwise_and, op1=ALU.logical_shift_left)
                v.tensor_tensor(bits[:, 0:cw], t1[:, 0:cw], t2[:, 0:cw], ALU.add)
                vq.inc(v.scalar_tensor_tensor(deqB[:, 0:cw], bits[:, 0:cw].bitcast(F32),
                                              R126, scp[:, 0:cw],
                                              op0=ALU.mult, op1=ALU.mult), 1)

    @block.gpsimd
    def _(g):
        st.wait(g)

    p.done()
    return nc



_CAST_SPECS = [
    ("L0_attn_w_qkv", 0), ("L0_attn_w_out", 1),
    ("L1_attn_Wq", 0), ("L1_attn_Wk", 0), ("L1_attn_Wv", 0), ("L1_attn_Wproj", 1),
    ("L1_attn_q_A", 0), ("L1_attn_k_A", 0), ("L1_attn_v_A", 0), ("L1_attn_p_A", None),
    ("L1_attn_q_B", None), ("L1_attn_k_B", None), ("L1_attn_v_B", None),
    ("L1_attn_p_B", 1),
]


def _prep_host_inputs(inp):
    wfc_p = np.ascontiguousarray(inp["L0_ffn_w_fc"][:, PI])
    wpj_p = np.ascontiguousarray(inp["L0_ffn_w_proj"][PI, :])

    shapes = []
    flat = []
    for name, ax in _CAST_SPECS:
        a = np.asarray(inp[name])
        if ax == 0:
            a = a[PI, :]
        elif ax == 1:
            a = a[:, PI]
        shapes.append(a.shape)
        flat.append(np.ascontiguousarray(a).reshape(-1))
    blob = np.concatenate(flat)
    n_tiles = -(-len(blob) // (128 * NQKV))
    n_tiles = -(-n_tiles // 8) * 8
    pad = n_tiles * 128 * NQKV - len(blob)
    blob = np.concatenate([blob, np.zeros(pad, np.float32)]).reshape(n_tiles * 128, NQKV)
    rows_pc = n_tiles // 8 * 128

    gp = np.ascontiguousarray(inp["L1_ffn_gate_W_packed"][PI_N, :].T)
    up = np.ascontiguousarray(inp["L1_ffn_up_W_packed"][PI_N, :].T)
    dp = np.ascontiguousarray(inp["L1_ffn_down_W_packed"][PI, :].T)
    gsc = np.ascontiguousarray(np.repeat(inp["L1_ffn_gate_scales"][PI_N, :].T, 16, axis=0))
    usc = np.ascontiguousarray(np.repeat(inp["L1_ffn_up_scales"][PI_N, :].T, 16, axis=0))
    dsc = np.ascontiguousarray(np.repeat(inp["L1_ffn_down_scales"][PI, :].T, 16, axis=0))

    maps = []
    for c in range(8):
        maps.append(dict(
            wfc_full=wfc_p, wpj_full=wpj_p,
            wfc_sl=np.ascontiguousarray(wfc_p[:, c * 256:(c + 1) * 256]),
            wpj_sl=np.ascontiguousarray(wpj_p[:, c * 640:(c + 1) * 640]),
            cast_in=np.ascontiguousarray(blob[c * rows_pc:(c + 1) * rows_pc]),
            g_pk=np.ascontiguousarray(gp[c * 128:(c + 1) * 128]),
            u_pk=np.ascontiguousarray(up[c * 128:(c + 1) * 128]),
            d_pk=np.ascontiguousarray(dp[c * 1024:(c + 1) * 1024]),
            g_sc=np.ascontiguousarray(gsc[c * 128:(c + 1) * 128]),
            u_sc=np.ascontiguousarray(usc[c * 128:(c + 1) * 128]),
            d_sc=np.ascontiguousarray(dsc[c * 1024:(c + 1) * 1024]),
        ))
    return maps, shapes, n_tiles // 8


def _prep_host_outputs(results, shapes):
    out = {}
    out["wfct"] = np.concatenate([r["wfct_o"] for r in results], 0)
    out["wpjt"] = np.concatenate([r["wpjt_o"] for r in results], 0)
    out["Wg_t"] = np.concatenate([r["g_o"] for r in results], 0)
    out["Wu_t"] = np.concatenate([r["u_o"] for r in results], 0)
    out["Wd_t"] = np.concatenate([r["do_o"] for r in results], 0)
    blob = np.concatenate([r["cast_o"] for r in results], 0).reshape(-1)
    pos = 0
    for (name, _), shp in zip(_CAST_SPECS, shapes):
        n = int(np.prod(shp))
        out[name] = blob[pos:pos + n].reshape(shp)
        pos += n
    return out



def emit_ln(p, block, x_sb, xn_sb, xb, sq, ssum, ssq, arow, brow, a_b, b_b,
            scr2, banks, ones1, xtmp, eps_b, after=None, upto=99):
    prep = p.sem()
    pe = p.sem()
    row = p.sem()
    bc = p.sem()
    dn = p.sem()
    sto = p.sem()

    def w(eng):
        for s_, v_ in (after or []):
            s_.wait(eng, v_)

    @block.scalar
    def _(s):
        s = W(s)
        w(s)
        for c in range(16):
            sl = slice(c * 1024, (c + 1) * 1024)
            s.activation(sq[:, sl], x_sb[:, sl], AF.Square)
            prep.inc(s.copy(xb[:, sl], x_sb[:, sl]), 1)

    if upto < 2:
        return prep
    @block.tensor
    def _(t):
        w(t)
        for tt in range(2):
            for c in range(16):
                prep.wait(t, c + 1)
                sl = slice(c * 1024 + tt * 512, c * 1024 + tt * 512 + 512)
                t.matmul(banks[0][0:1, 0:512], ones1[:], xb[:, sl],
                         start=(c == 0), stop=(c == 15))
            for c in range(16):
                sl = slice(c * 1024 + tt * 512, c * 1024 + tt * 512 + 512)
                inst = t.matmul(banks[1][0:1, 0:512], ones1[:], sq[:, sl],
                                start=(c == 0), stop=(c == 15))
            pe.inc(inst, 1)
            if tt == 0:
                row.wait(t, 1)

    if upto < 3:
        return pe
    @block.vector
    def _(v):
        v = W(v)
        w(v)
        for tt in range(2):
            pe.wait(v, tt + 1)
            sl = slice(tt * 512, tt * 512 + 512)
            v.tensor_copy(ssum[0:1, sl], banks[0][0:1, 0:512])
            row.inc(v.tensor_copy(ssq[0:1, sl], banks[1][0:1, 0:512]), 1)
        v.tensor_scalar(ssum[:], ssum[:], 1.0 / 2048.0, None, op0=ALU.mult)
        v.tensor_scalar(ssq[:], ssq[:], 1.0 / 2048.0, None, op0=ALU.mult)
        v.tensor_tensor(arow[:], ssum[:], ssum[:], ALU.mult)
        row.inc(v.tensor_tensor(arow[:], ssq[:], arow[:], ALU.subtract), 1)

    if upto < 4:
        return row
    @block.scalar
    def _(s):
        s = W(s)
        row.wait(s, 3)
        s.activation(arow[:], arow[:], AF.Ln, bias=eps_b[0:1, 0:1])
        row.inc(s.activation(arow[:], arow[:], AF.Exp, scale=-0.5), 1)

    @block.vector
    def _(v):
        v = W(v)
        row.wait(v, 4)
        row.inc(v.scalar_tensor_tensor(brow[:], ssum[:], -1.0, arow[:],
                                       op0=ALU.mult, op1=ALU.mult), 1)

    if upto < 5:
        return row
    @block.sync
    def _(sync):
        row.wait(sync, 5)
        sto.inc(sync.dma_start(out=scr2[0:1, :], in_=arow[:]))
        sto.inc(sync.dma_start(out=scr2[1:2, :], in_=brow[:]))
        sto.wait(sync, 32)
        bc.inc(sync.dma_start(out=a_b[:], in_=scr2[0:1, :].to_broadcast([128, 1024])))
        bc.inc(sync.dma_start(out=b_b[:], in_=scr2[1:2, :].to_broadcast([128, 1024])))

    if upto < 6:
        return bc
    @block.vector
    def _(v):
        v = W(v)
        bc.wait(v, 32)
        for c in range(16):
            sl = slice(c * 1024, (c + 1) * 1024)
            dn.inc(v.tensor_tensor(xtmp[:, 0:1024], x_sb[:, sl], a_b[:], ALU.mult), 1)
            dn.inc(v.tensor_tensor(xn_sb[:, sl], xtmp[:, 0:1024], b_b[:], ALU.add), 1)

    return dn


def emit_mm_stream(p, block, w_dram, nch, kch, rhs_fn, evac_fn, wbufs, wsem,
                   banks, n_tt=2, tt_cols=512, after=None, evac_engine="scalar",
                   extra_mm=None, single_buf=False):
    pegrp = p.sem()
    evs = p.sem()
    NB = len(banks)

    @block.sync
    def _(sync):
        for s_, v_ in (after or []):
            s_.wait(sync, v_)
        for n in range(nch):
            if single_buf and n >= 1:
                pegrp.wait(sync, n * n_tt)
            elif n >= 2:
                pegrp.wait(sync, (n - 1) * n_tt)
            wsem.inc(sync.dma_start(
                out=wbufs[n % 2][:].rearrange("p (k m) -> p k m", m=128),
                in_=w_dram[n]))

    @block.tensor
    def _(t):
        for s_, v_ in (after or []):
            s_.wait(t, v_)
        g = 0
        for n in range(nch):
            wsem.wait(t, 16 * (n + 1))
            for tt in range(n_tt):
                ps = banks[g % NB]
                if g >= NB:
                    evs.wait(t, g - NB + 1)
                for kc in range(kch):
                    last = (kc == kch - 1) and extra_mm is None
                    inst = t.matmul(ps[:, 0:tt_cols],
                                    wbufs[n % 2][:, kc * 128:(kc + 1) * 128],
                                    rhs_fn(kc, tt), start=(kc == 0), stop=last)
                if extra_mm is not None:
                    inst = extra_mm(t, n, tt, ps)
                pegrp.inc(inst, 1)
                g += 1

    body = block.scalar if evac_engine == "scalar" else block.vector

    def _evac(e):
        e = W(e)
        for s_, v_ in (after or []):
            s_.wait(e, v_)
        g = 0
        for n in range(nch):
            for tt in range(n_tt):
                pegrp.wait(e, g + 1)
                last = evac_fn(e, g, n, tt, banks[g % NB])
                evs.inc(last, 1)
                g += 1
    body(_evac)
    return pegrp, evs



def build_a0a(stages=('ln', 'qkv', 'v')):
    p = Prog()
    nc = p.nc
    dt = nc.dram_tensor

    xT = dt("xT", [D, T], F32, kind="ExternalInput").ap()
    wqk_t = dt("wqk_t", [32, 128, 16, 128], BF16, kind="ExternalInput").ap()
    wv_r = dt("wv_r", [8, 128, 16, 256], BF16, kind="ExternalInput").ap()
    q_o = dt("q_o", [D, T], BF16, kind="ExternalOutput").ap()
    k_o = dt("k_o", [D, T], BF16, kind="ExternalOutput").ap()
    v_o = dt("v_o", [T, D], BF16, kind="ExternalOutput").ap()
    scr2 = dt("scr2", [2, 1024], F32, kind="Internal").ap()

    x_sb = p.sbuf("x_sb", [128, 16384], F32)[:]
    xn = p.sbuf("xn", [128, 16384], BF16)[:]
    xbq = p.sbuf("xbq", [128, 16384], BF16)[:]
    sqk = p.sbuf("sqk", [128, 16384], BF16)[:]
    wb0 = p.sbuf("wb0", [128, 2048], BF16)
    wb1 = p.sbuf("wb1", [128, 2048], BF16)
    wvb = p.sbuf("wvb", [128, 4096], BF16)[:]
    ssum = p.sbuf("ssum", [1, 1024], F32)
    ssq = p.sbuf("ssq", [1, 1024], F32)
    arow = p.sbuf("arow", [1, 1024], F32)
    brow = p.sbuf("brow", [1, 1024], F32)
    a_b = p.sbuf("a_b", [128, 1024], F32)
    b_b = p.sbuf("b_b", [128, 1024], F32)
    xtmp = p.sbuf("xtmp", [128, 1024], F32)[:]
    ones1 = p.sbuf("ones1", [128, 1], BF16)
    banks = [p.psum(f"bank{i}", [128, 512]) for i in range(8)]

    ld = p.sem()
    wsem = p.sem()
    vw = p.sem()
    vpe = p.sem()
    vev = p.sem()
    sto = p.sem()
    block = p.begin()

    v_sb = x_sb.bitcast(BF16)[:, 0:16384]

    @block.sync
    def _(sync):
        for c in range(16):
            ld.inc(sync.dma_start(out=x_sb[:, c * 1024:(c + 1) * 1024],
                                  in_=xT[c * 128:(c + 1) * 128, :]))

    eps_b = p.sbuf("eps_b", [128, 1], F32)

    @block.vector
    def _(v):
        v = W(v)
        v.memset(ones1[:], 1.0)
        v.memset(eps_b[:], LN_EPS)

    upto = 99
    for st_ in stages:
        if st_.startswith('ln') and len(st_) > 2:
            upto = int(st_[2:])
    if any(st_.startswith('ln') for st_ in stages) or 'qkv' in stages or 'v' in stages:
        dn = emit_ln(p, block, x_sb, xn, xbq, sqk, ssum, ssq, arow, brow, a_b, b_b,
                     scr2, banks[0:2], ones1, xtmp, eps_b, after=[(ld, 256)], upto=upto)
    else:
        dn = ld

    def rhs_fn(kc, tt):
        return xn[:, kc * 1024 + tt * 512: kc * 1024 + (tt + 1) * 512]

    def evac_qk(e, g, n, tt, ps):
        dst = (xbq if n < 16 else sqk)
        c = n % 16
        sl = slice(c * 1024 + tt * 512, c * 1024 + (tt + 1) * 512)
        return e.activation(dst[:, sl], ps[:, 0:512], AF.Copy,
                            scale=(ISCALE if n < 16 else 1.0))

    if 'qkv' in stages:
        pegrp, evs = emit_mm_stream(
            p, block, wqk_t, 32, 16, rhs_fn, evac_qk, [wb0, wb1], wsem,
            banks[2:6], after=[(dn, 32)], evac_engine="scalar")

    if 'v' in stages:
        @block.sync
        def _(sync):
            dn.wait(sync, 32)
            for ng in range(8):
                if ng >= 1:
                    vpe.wait(sync, ng * 8)
                vw.inc(sync.dma_start(out=wvb.rearrange("p (c m) -> p c m", m=256),
                                      in_=wv_r[ng]))

        @block.tensor
        def _(t):
            for ng in range(8):
                vw.wait(t, 16 * (ng + 1))
                for tb in range(8):
                    g = ng * 8 + tb
                    ps = banks[6 + (g % 2)]
                    if g >= 2:
                        vev.wait(t, g - 1)
                    for dc in range(16):
                        inst = t.matmul(ps[:, 0:256],
                                        xn[:, dc * 1024 + tb * 128: dc * 1024 + (tb + 1) * 128],
                                        wvb[:, dc * 256:(dc + 1) * 256],
                                        start=(dc == 0), stop=(dc == 15))
                    vpe.inc(inst, 1)

        @block.vector
        def _(v):
            v = W(v)
            for g in range(64):
                ng, tb = divmod(g, 8)
                vpe.wait(v, g + 1)
                vev.inc(v.tensor_copy(
                    v_sb[:, tb * 2048 + ng * 256: tb * 2048 + (ng + 1) * 256],
                    banks[6 + (g % 2)][:, 0:256]), 1)

    @block.gpsimd
    def _(g):
        evs.wait(g, 64)
        for c in range(16):
            sto.inc(g.dma_start(out=q_o[c * 128:(c + 1) * 128, :],
                                in_=xbq[:, c * 1024:(c + 1) * 1024]))
            sto.inc(g.dma_start(out=k_o[c * 128:(c + 1) * 128, :],
                                in_=sqk[:, c * 1024:(c + 1) * 1024]))
        vev.wait(g, 64)
        for b2 in range(8):
            sto.inc(g.dma_start(out=v_o[b2 * 128:(b2 + 1) * 128, :],
                                in_=v_sb[:, b2 * 2048:(b2 + 1) * 2048]))
        sto.wait(g)

    p.done()
    return nc



def emit_postln(p, block, y_bf, x_sb, ssum, ssq, arow, brow, a_b, b_b,
                scr2, banks, ones1, xtmp, eps_b, sqs, after=None):
    pe = p.sem()
    row = p.sem()
    bc = p.sem()
    dn = p.sem()
    sto = p.sem()
    sqpe = p.sem()
    sqrdy = p.sem()

    def w(eng):
        for s_, v_ in (after or []):
            s_.wait(eng, v_)

    @block.scalar
    def _(s):
        s = W(s)
        w(s)
        k = 0
        for tt in range(2):
            for c in range(16):
                if k >= 2:
                    sqpe.wait(s, k - 1)
                sl = slice(c * 1024 + tt * 512, c * 1024 + tt * 512 + 512)
                sqrdy.inc(s.activation(sqs[k % 2][:], y_bf[:, sl], AF.Square), 1)
                k += 1

    @block.tensor
    def _(t):
        w(t)
        k = 0
        for tt in range(2):
            for c in range(16):
                sqrdy.wait(t, k + 1)
                sl = slice(c * 1024 + tt * 512, c * 1024 + tt * 512 + 512)
                t.matmul(banks[0][0:1, 0:512], ones1[:], y_bf[:, sl],
                         start=(c == 0), stop=(c == 15))
                inst = t.matmul(banks[1][0:1, 0:512], ones1[:], sqs[k % 2][:],
                                start=(c == 0), stop=(c == 15))
                sqpe.inc(inst, 1)
                k += 1
            if tt == 0:
                row.wait(t, 1)

    @block.vector
    def _(v):
        v = W(v)
        w(v)
        for tt in range(2):
            sqpe.wait(v, 16 * (tt + 1))
            sl = slice(tt * 512, tt * 512 + 512)
            v.tensor_copy(ssum[0:1, sl], banks[0][0:1, 0:512])
            row.inc(v.tensor_copy(ssq[0:1, sl], banks[1][0:1, 0:512]), 1)
        v.tensor_scalar(ssum[:], ssum[:], 1.0 / 2048.0, None, op0=ALU.mult)
        v.tensor_scalar(ssq[:], ssq[:], 1.0 / 2048.0, None, op0=ALU.mult)
        v.tensor_tensor(arow[:], ssum[:], ssum[:], ALU.mult)
        row.inc(v.tensor_tensor(arow[:], ssq[:], arow[:], ALU.subtract), 1)

    @block.scalar
    def _(s):
        s = W(s)
        row.wait(s, 3)
        s.activation(arow[:], arow[:], AF.Ln, bias=eps_b[0:1, 0:1])
        row.inc(s.activation(arow[:], arow[:], AF.Exp, scale=-0.5), 1)

    @block.vector
    def _(v):
        v = W(v)
        row.wait(v, 4)
        row.inc(v.scalar_tensor_tensor(brow[:], ssum[:], -1.0, arow[:],
                                       op0=ALU.mult, op1=ALU.mult), 1)

    @block.sync
    def _(sync):
        row.wait(sync, 5)
        sto.inc(sync.dma_start(out=scr2[0:1, :], in_=arow[:]))
        sto.inc(sync.dma_start(out=scr2[1:2, :], in_=brow[:]))
        sto.wait(sync, 32)
        bc.inc(sync.dma_start(out=a_b[:], in_=scr2[0:1, :].to_broadcast([128, 1024])))
        bc.inc(sync.dma_start(out=b_b[:], in_=scr2[1:2, :].to_broadcast([128, 1024])))

    @block.vector
    def _(v):
        v = W(v)
        bc.wait(v, 32)
        for c in range(16):
            sl = slice(c * 1024, (c + 1) * 1024)
            dn.inc(v.tensor_tensor(xtmp[:], y_bf[:, sl], a_b[:], ALU.mult), 1)
            dn.inc(v.tensor_tensor(xtmp[:], xtmp[:], b_b[:], ALU.add), 1)
            dn.inc(v.tensor_tensor(x_sb[:, sl], x_sb[:, sl], xtmp[:], ALU.add), 1)

    return dn


def build_a0b():
    p = Prog()
    nc = p.nc
    dt = nc.dram_tensor

    q_in = dt("q_in", [D, T], BF16, kind="ExternalInput").ap()
    kT = dt("kT", [D, S], BF16, kind="ExternalInput").ap()
    v_pad = dt("v_pad", [S, HEADS * 129], BF16, kind="ExternalInput").ap()
    mask = dt("mask", [T, S], BF16, kind="ExternalInput").ap()
    xT = dt("xT", [D, T], F32, kind="ExternalInput").ap()
    wout_t = dt("wout_t", [16, 128, 16, 128], BF16, kind="ExternalInput").ap()
    ident_in = dt("ident_in", [128, 128], F32, kind="ExternalInput").ap()
    x1_o = dt("x1_o", [D, T], F32, kind="ExternalOutput").ap()
    scr2 = dt("scr2", [2, 1024], F32, kind="Internal").ap()

    q_sb = p.sbuf("q_sb", [128, 16384], BF16)[:]
    r1 = p.sbuf("r1", [128, 33024], BF16)[:]
    y_tm = p.sbuf("y_tm", [128, 16384], BF16)[:]
    kb = [p.sbuf(f"kb{i}", [128, 2048], BF16) for i in range(2)]
    msk = p.sbuf("msk", [128, 2048], BF16)
    P = p.sbuf("P", [128, 2048], BF16)[:]
    PT = p.sbuf("PT", [128, 2048], BF16)[:]
    identb = p.sbuf("identb", [128, 128], BF16)
    idf = p.sbuf("idf", [128, 128], F32)
    nm4 = p.sbuf("nm4", [128, 4], F32)
    nm1 = p.sbuf("nm1", [128, 1], F32)
    rec = p.sbuf("rec", [128, 1], F32)
    wb0 = p.sbuf("wb0", [128, 2048], BF16)
    wb1 = p.sbuf("wb1", [128, 2048], BF16)
    ssum = p.sbuf("ssum", [1, 1024], F32)
    ssq = p.sbuf("ssq", [1, 1024], F32)
    arow = p.sbuf("arow", [1, 1024], F32)
    brow = p.sbuf("brow", [1, 1024], F32)
    a_b = p.sbuf("a_b", [128, 1024], F32)
    b_b = p.sbuf("b_b", [128, 1024], F32)
    xtmp = p.sbuf("xtmp", [128, 1024], F32)
    sqs = [p.sbuf(f"sqs{i}", [128, 512], BF16) for i in range(2)]
    ones1 = p.sbuf("ones1", [128, 1], BF16)
    eps_b = p.sbuf("eps_b", [128, 1], F32)
    banks = [p.psum(f"bank{i}", [128, 512]) for i in range(8)]

    x_sb = r1.bitcast(F32)[:, 0:16384]

    ldc = p.sem()
    mld = p.sem()
    kld = p.sem()
    spe = p.sem()
    smax = p.sem()
    sexp = p.sem()
    tpe = p.sem()
    tev = p.sem()
    ape = p.sem()
    avd = p.sem()
    xld = p.sem()
    tbpe = p.sem()
    tbev = p.sem()
    wsem = p.sem()
    sto = p.sem()
    block = p.begin()

    @block.sync
    def _(sync):
        ldc.inc(sync.dma_start(out=idf[:], in_=ident_in))
        for c in range(16):
            ldc.inc(sync.dma_start(out=q_sb[:, c * 1024:(c + 1) * 1024],
                                   in_=q_in[c * 128:(c + 1) * 128, :]))
        for j in range(16):
            ldc.inc(sync.dma_start(out=r1[:, j * 2064:(j + 1) * 2064],
                                   in_=v_pad[j * 128:(j + 1) * 128, :]))
        for qt in range(8):
            if qt >= 1:
                spe.wait(sync, 64 * qt)
            mld.inc(sync.dma_start(out=msk[:], in_=mask[qt * 128:(qt + 1) * 128, :]))
            for h in range(16):
                g = qt * 16 + h
                if g >= 2:
                    spe.wait(sync, 4 * (g - 1))
                kld.inc(sync.dma_start(out=kb[g % 2][:],
                                       in_=kT[h * 128:(h + 1) * 128, :]))

    @block.vector
    def _(v):
        v = W(v)
        v.memset(ones1[:], 1.0)
        v.memset(eps_b[:], LN_EPS)
        ldc.wait(v, 16)
        v.tensor_copy(identb[:], idf[:])

    @block.tensor
    def _(t):
        ldc.wait(t)
        for qt in range(8):
            for h in range(16):
                g = qt * 16 + h
                kld.wait(t, 16 * (g + 1))
                mld.wait(t, 16 * (qt + 1))
                if g >= 1:
                    sexp.wait(t, 4 * g)
                    avd.wait(t, g)
                lhs = q_sb[:, h * 1024 + qt * 128: h * 1024 + (qt + 1) * 128]
                for bnk in range(4):
                    t.matmul(banks[bnk][:, 0:512], lhs,
                             kb[g % 2][:, bnk * 512:(bnk + 1) * 512],
                             start=True, stop=False)
                    spe.inc(t.matmul(banks[bnk][:, 0:512], identb[:],
                                     msk[:, bnk * 512:(bnk + 1) * 512],
                                     start=False, stop=True), 1)
                if g >= 1:
                    tev.wait(t, 4 * g)
                sexp.wait(t, 4 * (g + 1))
                for j in range(16):
                    inst = t.matmul(banks[4 + j // 4][:, (j % 4) * 128:(j % 4) * 128 + 128],
                                    P[:, j * 128:(j + 1) * 128], identb[:],
                                    start=True, stop=True)
                    if j % 4 == 3:
                        tpe.inc(inst, 1)
                tev.wait(t, 4 * (g + 1))
                if g >= 1:
                    avd.wait(t, g)
                for j in range(16):
                    inst = t.matmul(banks[0][:, 0:129],
                                    PT[:, j * 128:(j + 1) * 128],
                                    r1[:, j * 2064 + h * 129: j * 2064 + (h + 1) * 129],
                                    start=(j == 0), stop=(j == 15))
                ape.inc(inst, 1)

    @block.vector
    def _(v):
        v = W(v)
        for g in range(128):
            spe.wait(v, 4 * (g + 1))
            for bnk in range(4):
                v.tensor_reduce(nm4[:, bnk:bnk + 1], banks[bnk][:, 0:512],
                                axis=AX.X, op=ALU.max, negate=True)
                v.drain()
            v.tensor_reduce(nm1[:], nm4[:], axis=AX.X, op=ALU.min)
            v.drain()
            smax.inc(v.nop(), 1)
            ape.wait(v, g + 1)
            qt, h = divmod(g, 16)
            v.reciprocal(rec[:], banks[0][:, 128:129])
            v.drain()
            avd.inc(v.tensor_scalar(
                y_tm[:, qt * 2048 + h * 128: qt * 2048 + (h + 1) * 128],
                banks[0][:, 0:128], rec[:], None, op0=ALU.mult), 1)

    @block.scalar
    def _(s):
        s = W(s)
        for g in range(128):
            smax.wait(s, g + 1)
            if g >= 1:
                tpe.wait(s, 4 * g)
            for bnk in range(4):
                sexp.inc(s.activation(P[:, bnk * 512:(bnk + 1) * 512],
                                      banks[bnk][:, 0:512], AF.Exp,
                                      bias=nm1[:]), 1)
            tpe.wait(s, 4 * (g + 1))
            if g >= 1:
                ape.wait(s, g)
            for bg in range(4):
                tev.inc(s.copy(PT[:, bg * 512:(bg + 1) * 512],
                               banks[4 + bg][:, 0:512]), 1)

    @block.sync
    def _(sync):
        ape.wait(sync, 128)
        for c in range(16):
            xld.inc(sync.dma_start(out=x_sb[:, c * 1024:(c + 1) * 1024],
                                   in_=xT[c * 128:(c + 1) * 128, :]))

    y_fm = q_sb

    @block.tensor
    def _(t):
        avd.wait(t, 128)
        for grp in range(32):
            if grp >= 2:
                tbev.wait(t, 4 * (grp - 1))
            for jj in range(4):
                j = grp * 4 + jj
                qt, fc = divmod(j, 16)
                inst = t.matmul(banks[4 + grp % 2][:, jj * 128:(jj + 1) * 128],
                                y_tm[:, qt * 2048 + fc * 128: qt * 2048 + (fc + 1) * 128],
                                identb[:], start=True, stop=True)
            tbpe.inc(inst, 1)

    @block.scalar
    def _(s):
        s = W(s)
        for grp in range(32):
            tbpe.wait(s, grp + 1)
            for jj in range(4):
                j = grp * 4 + jj
                qt, fc = divmod(j, 16)
                tbev.inc(s.copy(
                    y_fm[:, fc * 1024 + qt * 128: fc * 1024 + (qt + 1) * 128],
                    banks[4 + grp % 2][:, jj * 128:(jj + 1) * 128]), 1)

    proj_bf = y_tm

    def rhs_fn(kc, tt):
        return y_fm[:, kc * 1024 + tt * 512: kc * 1024 + (tt + 1) * 512]

    def evac_proj(e, g, n, tt, ps):
        return e.copy(proj_bf[:, n * 1024 + tt * 512: n * 1024 + (tt + 1) * 512],
                      ps[:, 0:512])

    pegrp, evs = emit_mm_stream(
        p, block, wout_t, 16, 16, rhs_fn, evac_proj, [wb0, wb1], wsem,
        banks[0:4], after=[(tbev, 128)], evac_engine="scalar")

    dn = emit_postln(p, block, proj_bf[:, 0:16384], x_sb, ssum, ssq, arow, brow,
                     a_b, b_b, scr2, banks[4:6], ones1, xtmp, eps_b, sqs,
                     after=[(evs, 32), (xld, 256)])

    @block.gpsimd
    def _(g):
        dn.wait(g, 48)
        for c in range(16):
            sto.inc(g.dma_start(out=x1_o[c * 128:(c + 1) * 128, :],
                                in_=x_sb[:, c * 1024:(c + 1) * 1024]))
        sto.wait(g)

    p.done()
    return nc



def build_f0():
    p = Prog()
    nc = p.nc
    dt = nc.dram_tensor

    xT = dt("xT", [D, T], F32, kind="ExternalInput").ap()
    wfc_t = dt("wfc_t", [40, 128, 16, 128], BF16, kind="ExternalInput").ap()
    wpj_t = dt("wpj_t", [16, 128, 40, 128], BF16, kind="ExternalInput").ap()
    x2_o = dt("x2_o", [D, T], F32, kind="ExternalOutput").ap()
    scr2 = dt("scr2", [2, 1024], F32, kind="Internal").ap()

    x_sb = p.sbuf("x_sb", [128, 16384], F32)[:]
    xn = p.sbuf("xn", [128, 16384], BF16)[:]
    h_bf = p.sbuf("h_bf", [128, 40960], BF16)[:]
    y_bf = xn
    wb0 = p.sbuf("wb0", [128, 2048], BF16)
    wb1 = p.sbuf("wb1", [128, 2048], BF16)
    wbP = p.sbuf("wbP", [128, 5120], BF16)
    rows4 = p.sbuf("rows4", [1, 3072], F32)
    ssum = rows4[0:1, 0:1024]
    ssq = rows4[0:1, 1024:2048]
    arow = rows4[0:1, 2048:3072]
    brow = rows4[0:1, 1024:2048]
    wbPf = wbP[:].bitcast(F32)
    a_b = wbPf[:, 0:1024]
    b_b = wbPf[:, 1024:2048]
    relu_t = wbPf[:, 2048:2560]
    xtmp = h_bf.bitcast(F32)[:, 0:1024]
    sqs = [wb0[:, 0:512], wb1[:, 0:512]]
    ones1 = p.sbuf("ones1", [128, 1], BF16)
    eps_b = p.sbuf("eps_b", [128, 1], F32)
    banks = [p.psum(f"bank{i}", [128, 512]) for i in range(8)]

    xb = h_bf[:, 0:16384]
    sq = h_bf[:, 16384:32768]

    ld = p.sem()
    w1 = p.sem()
    w2 = p.sem()
    sto = p.sem()
    block = p.begin()

    @block.sync
    def _(sync):
        for c in range(16):
            ld.inc(sync.dma_start(out=x_sb[:, c * 1024:(c + 1) * 1024],
                                  in_=xT[c * 128:(c + 1) * 128, :]))

    @block.vector
    def _(v):
        v = W(v)
        v.memset(ones1[:], 1.0)
        v.memset(eps_b[:], LN_EPS)

    dn = emit_ln(p, block, x_sb, xn, xb, sq, ssum, ssq, arow, brow, a_b, b_b,
                 scr2, banks[0:2], ones1, xtmp, eps_b, after=[(ld, 256)])

    def rhs_fc(kc, tt):
        return xn[:, kc * 1024 + tt * 512: kc * 1024 + (tt + 1) * 512]

    def evac_fc(e, g, n, tt, ps):
        e.activation(relu_t[:], ps[:, 0:512], AF.Relu)
        return e.activation(h_bf[:, n * 1024 + tt * 512: n * 1024 + (tt + 1) * 512],
                            relu_t[:], AF.Square)

    pegrp1, evs1 = emit_mm_stream(
        p, block, wfc_t, 40, 16, rhs_fc, evac_fc, [wb0, wb1], w1,
        banks[2:6], after=[(dn, 32)], evac_engine="scalar")

    def rhs_pj(kc, tt):
        return h_bf[:, kc * 1024 + tt * 512: kc * 1024 + (tt + 1) * 512]

    def evac_pj(e, g, n, tt, ps):
        return e.copy(y_bf[:, n * 1024 + tt * 512: n * 1024 + (tt + 1) * 512],
                      ps[:, 0:512])

    pegrp2, evs2 = emit_mm_stream(
        p, block, wpj_t, 16, 40, rhs_pj, evac_pj, [wbP, wbP], w2,
        banks[2:6], after=[(evs1, 80)], evac_engine="scalar", single_buf=True)

    dn2 = emit_postln(p, block, y_bf, x_sb, ssum, ssq, arow, brow,
                      a_b, b_b, scr2, banks[6:8], ones1, xtmp, eps_b, sqs,
                      after=[(evs2, 32)])

    @block.gpsimd
    def _(g):
        dn2.wait(g, 48)
        for c in range(16):
            sto.inc(g.dma_start(out=x2_o[c * 128:(c + 1) * 128, :],
                                in_=x_sb[:, c * 1024:(c + 1) * 1024]))
        sto.wait(g)

    p.done()
    return nc



def build_a1a():
    p = Prog()
    nc = p.nc
    dt = nc.dram_tensor

    xT = dt("xT", [D, T], F32, kind="ExternalInput").ap()
    wq_t = dt("wq_t", [16, 128, 16, 128], BF16, kind="ExternalInput").ap()
    wk_t = dt("wk_t", [4, 128, 16, 128], BF16, kind="ExternalInput").ap()
    wv_r = dt("wv_r", [2, 128, 16, 256], BF16, kind="ExternalInput").ap()
    qa_in = dt("qa_in", [128, 256], BF16, kind="ExternalInput").ap()
    ka_in = dt("ka_in", [128, 256], BF16, kind="ExternalInput").ap()
    va_in = dt("va_in", [128, 256], BF16, kind="ExternalInput").ap()
    qb_in = dt("qb_in", [16, D], BF16, kind="ExternalInput").ap()
    kb_in = dt("kb_in", [16, KVD], BF16, kind="ExternalInput").ap()
    vb_in = dt("vb_in", [16, KVD], BF16, kind="ExternalInput").ap()
    gain_in = dt("gain_in", [1, 16], F32, kind="ExternalInput").ap()
    cs_in = dt("cs_in", [256, 1024], F32, kind="ExternalInput").ap()
    q_o = dt("q_o", [D, T], BF16, kind="ExternalOutput").ap()
    k_o = dt("k_o", [KVD, T], BF16, kind="ExternalOutput").ap()
    v_o = dt("v_o", [T, KVD], BF16, kind="ExternalOutput").ap()
    scr2 = dt("scr2", [2, 1024], F32, kind="Internal").ap()

    x_sb = p.sbuf("x_sb", [128, 16384], F32)[:]
    xn = p.sbuf("xn", [128, 16384], BF16)[:]
    q_bf = p.sbuf("q_bf", [128, 16384], BF16)[:]
    k_bf = p.sbuf("k_bf", [128, 4096], BF16)[:]
    v_sb = p.sbuf("v_sb", [128, 4096], BF16)[:]
    wb0 = p.sbuf("wb0", [128, 2048], BF16)
    wb1 = p.sbuf("wb1", [128, 2048], BF16)
    wvb = p.sbuf("wvb", [128, 4096], BF16)[:]
    Cs = p.sbuf("Cs", [128, 1024], F32)
    Ss = p.sbuf("Ss", [128, 1024], F32)
    tmp1 = p.sbuf("tmp1", [128, 1024], F32)
    qp = p.sbuf("qp", [128, 1024], BF16)
    rb = p.sbuf("rb", [128, 1024], F32)
    qa_sb = p.sbuf("qa_sb", [128, 256], BF16)
    ka_sb = p.sbuf("ka_sb", [128, 256], BF16)
    va_sb = p.sbuf("va_sb", [128, 256], BF16)
    qb_sb = p.sbuf("qb_sb", [16, D], BF16)
    kb_sb = p.sbuf("kb_sb", [16, KVD], BF16)
    vb_sb = p.sbuf("vb_sb", [16, KVD], BF16)
    tq_sb = p.sbuf("tq_sb", [16, 1024], BF16)
    tk_sb = p.sbuf("tk_sb", [16, 1024], BF16)
    tv_sb = p.sbuf("tv_sb", [16, 1024], BF16)
    g2 = p.sbuf("g2", [1, 16], F32)
    rows4 = p.sbuf("rows4", [1, 3072], F32)
    ssum = rows4[0:1, 0:1024]
    ssq = rows4[0:1, 1024:2048]
    arow = rows4[0:1, 2048:3072]
    brow = rows4[0:1, 1024:2048]
    sqh = wb1[:, 0:1024]
    tmp2 = wb0[:].bitcast(F32)[:, 0:1024]
    a_b = v_sb.bitcast(F32)[:, 0:1024]
    b_b = v_sb.bitcast(F32)[:, 1024:2048]
    ones1 = p.sbuf("ones1", [128, 1], BF16)
    eps_b = p.sbuf("eps_b", [128, 1], F32)
    rms_b = p.sbuf("rms_b", [128, 1], F32)
    banks = [p.psum(f"bank{i}", [128, 512]) for i in range(8)]
    xtmp = tmp1[:]

    ld = p.sem()
    ldc = p.sem()
    wq_s = p.sem()
    wk_s = p.sem()
    vw = p.sem()
    vpe = p.sem()
    vev = p.sem()
    sto = p.sem()
    block = p.begin()

    @block.sync
    def _(sync):
        for c in range(16):
            ld.inc(sync.dma_start(out=x_sb[:, c * 1024:(c + 1) * 1024],
                                  in_=xT[c * 128:(c + 1) * 128, :]))
        ldc.inc(sync.dma_start(out=Cs[:], in_=cs_in[0:128, :]))
        ldc.inc(sync.dma_start(out=Ss[:], in_=cs_in[128:256, :]))
        ldc.inc(sync.dma_start(out=qa_sb[:], in_=qa_in))
        ldc.inc(sync.dma_start(out=ka_sb[:], in_=ka_in))
        ldc.inc(sync.dma_start(out=va_sb[:], in_=va_in))
        ldc.inc(sync.dma_start(out=qb_sb[:], in_=qb_in))
        ldc.inc(sync.dma_start(out=kb_sb[:], in_=kb_in))
        ldc.inc(sync.dma_start(out=vb_sb[:], in_=vb_in))
        ldc.inc(sync.dma_start(out=g2[:], in_=gain_in))

    @block.vector
    def _(v):
        v = W(v)
        v.memset(ones1[:], 1.0)
        v.memset(eps_b[:], 1.0)
        v.memset(rms_b[:], RMS_EPS)
        v.memset(eps_b[:], LN_EPS)
        v.memset(qp[:], 0.0)
        ldc.wait(v, 16 * 9)
        v.tensor_scalar(g2[:], g2[:], ISCALE, None, op0=ALU.mult)

    dn = emit_ln(p, block, x_sb, xn, q_bf[:, 0:16384], xn, ssum, ssq,
                 arow, brow, a_b, b_b, scr2, banks[0:2], ones1, xtmp, eps_b,
                 after=[(ld, 256)])

    lrpe = p.sem()
    lrev = p.sem()

    @block.tensor
    def _(t):
        dn.wait(t, 32)
        for i2, asb in enumerate([qa_sb, ka_sb, va_sb]):
            for tt in range(2):
                for dc in range(16):
                    inst = t.matmul(banks[2][0:16, 0:512],
                                    asb[:, dc * 16:(dc + 1) * 16],
                                    xn[:, dc * 1024 + tt * 512: dc * 1024 + (tt + 1) * 512],
                                    start=(dc == 0), stop=(dc == 15))
                lrpe.inc(inst, 1)
                lrev.wait(t, 2 * i2 + tt + 1)

    @block.scalar
    def _(s):
        s = W(s)
        for i2, tsb in enumerate([tq_sb, tk_sb, tv_sb]):
            for tt in range(2):
                lrpe.wait(s, 2 * i2 + tt + 1)
                lrev.inc(s.copy(tsb[0:16, tt * 512:(tt + 1) * 512],
                                banks[2][0:16, 0:512]), 1)

    def rhs_fn(kc, tt):
        return xn[:, kc * 1024 + tt * 512: kc * 1024 + (tt + 1) * 512]

    def mk_extra(bsb, tsb):
        def extra(t, n, tt, ps):
            return t.matmul(ps[:, 0:512], bsb[0:16, n * 128:(n + 1) * 128],
                            tsb[0:16, tt * 512:(tt + 1) * 512],
                            start=False, stop=True)
        return extra

    def evac_q(e, g, n, tt, ps):
        return e.copy(q_bf[:, n * 1024 + tt * 512: n * 1024 + (tt + 1) * 512],
                      ps[:, 0:512])

    qpe, qev = emit_mm_stream(
        p, block, wq_t, 16, 16, rhs_fn, evac_q, [wb0, wb1], wq_s,
        banks[3:6], after=[(lrev, 6)], evac_engine="scalar",
        extra_mm=mk_extra(qb_sb, tq_sb))

    def evac_k(e, g, n, tt, ps):
        return e.copy(k_bf[:, n * 1024 + tt * 512: n * 1024 + (tt + 1) * 512],
                      ps[:, 0:512])

    kpe, kev = emit_mm_stream(
        p, block, wk_t, 4, 16, rhs_fn, evac_k, [wb0, wb1], wk_s,
        banks[3:6], after=[(qev, 32)], evac_engine="scalar",
        extra_mm=mk_extra(kb_sb, tk_sb))

    @block.sync
    def _(sync):
        dn.wait(sync, 32)
        for ng in range(2):
            if ng >= 1:
                vpe.wait(sync, ng * 8)
            vw.inc(sync.dma_start(out=wvb.rearrange("p (c m) -> p c m", m=256),
                                  in_=wv_r[ng]))

    @block.tensor
    def _(t):
        kev.wait(t, 8)
        for ng in range(2):
            vw.wait(t, 16 * (ng + 1))
            for tb in range(8):
                g = ng * 8 + tb
                ps = banks[6 + (g % 2)]
                if g >= 2:
                    vev.wait(t, g - 1)
                for dc in range(16):
                    t.matmul(ps[:, 0:256],
                             xn[:, dc * 1024 + tb * 128: dc * 1024 + (tb + 1) * 128],
                             wvb[:, dc * 256:(dc + 1) * 256],
                             start=(dc == 0), stop=False)
                inst = t.matmul(ps[:, 0:256],
                                tv_sb[0:16, tb * 128:(tb + 1) * 128],
                                vb_sb[0:16, ng * 256:(ng + 1) * 256],
                                start=False, stop=True)
                vpe.inc(inst, 1)

    @block.vector
    def _(v):
        v = W(v)
        for g in range(16):
            ng, tb = divmod(g, 8)
            vpe.wait(v, g + 1)
            vev.inc(v.tensor_copy(
                v_sb[:, tb * 512 + ng * 256: tb * 512 + (ng + 1) * 256],
                banks[6 + (g % 2)][:, 0:256]), 1)

    heads = [("q", h) for h in range(16)] + [("k", h) for h in range(4)]
    hsq = p.sem()
    hst = p.sem()
    hrow = p.sem()
    hexp = p.sem()
    hmul = p.sem()
    hwr = p.sem()
    hrd = p.sem()
    hdn = p.sem()

    @block.scalar
    def _(s):
        s = W(s)
        qev.wait(s, 32)
        kev.wait(s, 8)
        for hi, (kind, h) in enumerate(heads):
            src = q_bf if kind == "q" else k_bf
            if hi >= 1:
                hst.wait(s, 2 * hi)
            hsq.inc(s.activation(sqh[:], src[:, h * 1024:(h + 1) * 1024],
                                 AF.Square), 1)
            hrow.wait(s, hi + 1)
            s.activation(arow[:], arow[:], AF.Ln, scale=1.0 / 128.0,
                         bias=rms_b[0:1, 0:1])
            hexp.inc(s.activation(arow[:], arow[:], AF.Exp, scale=-0.5), 1)

    @block.tensor
    def _(t):
        qev.wait(t, 32)
        kev.wait(t, 8)
        for hi, (kind, h) in enumerate(heads):
            hsq.wait(t, hi + 1)
            if hi >= 1:
                hrow.wait(t, hi)
            for tt in range(2):
                inst = t.matmul(banks[2 + tt][0:1, 0:512], ones1[:],
                                sqh[:, tt * 512:(tt + 1) * 512],
                                start=True, stop=True)
                hst.inc(inst, 1)

    @block.vector
    def _(v):
        v = W(v)
        for hi, (kind, h) in enumerate(heads):
            src = q_bf if kind == "q" else k_bf
            sl = slice(h * 1024, (h + 1) * 1024)
            if hi >= 1:
                hwr.wait(v, 16 * hi)
            hst.wait(v, 2 * hi + 2)
            v.tensor_copy(arow[0:1, 0:512], banks[2][0:1, 0:512])
            hrow.inc(v.tensor_copy(arow[0:1, 512:1024], banks[3][0:1, 0:512]), 1)
            hexp.wait(v, hi + 1)
            if kind == "q":
                hmul.inc(v.tensor_scalar(arow[:], arow[:], g2[0:1, h:h + 1],
                                         None, op0=ALU.mult), 1)
            else:
                hmul.inc(v.tensor_copy(arow[:], arow[:]), 1)
            hrd.wait(v, 48 * (hi + 1))
            v.tensor_tensor(tmp1[:], src[:, sl], Cs[:], ALU.mult)
            v.tensor_tensor(tmp2[:], qp[:], Ss[:], ALU.mult)
            v.tensor_tensor(tmp1[:], tmp1[:], tmp2[:], ALU.add)
            hdn.inc(v.tensor_tensor(src[:, sl], tmp1[:], rb[:], ALU.mult), 1)

    @block.sync
    def _(sync):
        for hi, (kind, h) in enumerate(heads):
            src = q_bf if kind == "q" else k_bf
            hmul.wait(sync, hi + 1)
            hwr.inc(sync.dma_start(out=scr2[0:1, :], in_=arow[:]))
            sync.wait_ge(hwr.h, hwr.n)
            if hi >= 1:
                hdn.wait(sync, hi)
            hrd.inc(sync.dma_start(out=rb[:], in_=scr2[0:1, :].to_broadcast([128, 1024])))
            hrd.inc(sync.dma_start(out=qp[0:32, :],
                                   in_=src[32:64, h * 1024:(h + 1) * 1024]))
            hrd.inc(sync.dma_start(out=qp[32:64, :],
                                   in_=src[0:32, h * 1024:(h + 1) * 1024]))

    @block.gpsimd
    def _(g):
        hdn.wait(g, 20)
        vev.wait(g, 16)
        for c in range(16):
            sto.inc(g.dma_start(out=q_o[c * 128:(c + 1) * 128, :],
                                in_=q_bf[:, c * 1024:(c + 1) * 1024]))
        for c in range(4):
            sto.inc(g.dma_start(out=k_o[c * 128:(c + 1) * 128, :],
                                in_=k_bf[:, c * 1024:(c + 1) * 1024]))
        for b2 in range(8):
            sto.inc(g.dma_start(out=v_o[b2 * 128:(b2 + 1) * 128, :],
                                in_=v_sb[:, b2 * 512:(b2 + 1) * 512]))
        sto.wait(g)

    p.done()
    return nc





def build_a1b():
    p = Prog()
    nc = p.nc
    dt = nc.dram_tensor

    q_in = dt("q_in", [D, T], BF16, kind="ExternalInput").ap()
    kT = dt("kT", [KVD, S], BF16, kind="ExternalInput").ap()
    v_pad = dt("v_pad", [S, HKV * 129], BF16, kind="ExternalInput").ap()
    v_own = dt("v_own", [T, HKV * 129], BF16, kind="ExternalInput").ap()
    mask = dt("mask", [T, S], BF16, kind="ExternalInput").ap()
    xT = dt("xT", [D, T], F32, kind="ExternalInput").ap()
    wproj_t = dt("wproj_t", [16, 128, 16, 128], BF16, kind="ExternalInput").ap()
    pa_in = dt("pa_in", [128, 256], BF16, kind="ExternalInput").ap()
    pb_in = dt("pb_in", [16, D], BF16, kind="ExternalInput").ap()
    ident_in = dt("ident_in", [128, 128], F32, kind="ExternalInput").ap()
    x3_o = dt("x3_o", [D, T], F32, kind="ExternalOutput").ap()
    scr2 = dt("scr2", [2, 1024], F32, kind="Internal").ap()

    X = p.sbuf("X", [128, 33024], BF16)[:]
    K_sb = X[:, 0:8192]
    vp_sb = X[:, 8192:16448]
    vo_sb = X[:, 16448:20576]
    msk = X[:, 20576:22624]
    P = X[:, 22624:24672]
    PT = X[:, 24672:26720]
    x_sb = X.bitcast(F32)[:, 0:16384]

    q_sb = p.sbuf("q_sb", [128, 16384], BF16)[:]
    y_tm = p.sbuf("y_tm", [128, 16384], BF16)[:]
    wb0 = p.sbuf("wb0", [128, 2048], BF16)
    wb1 = p.sbuf("wb1", [128, 2048], BF16)
    identb = p.sbuf("identb", [128, 128], BF16)
    idf = p.sbuf("idf", [128, 128], F32)
    nm4 = p.sbuf("nm4", [128, 4], F32)
    nm1 = p.sbuf("nm1", [128, 1], F32)
    rec = p.sbuf("rec", [128, 1], F32)
    vn_sb = p.sbuf("vn_sb", [128, 512], F32)
    ztmp = p.sbuf("ztmp", [128, 2048], F32)
    dot = p.sbuf("dot", [128, 16], F32)
    dnorm = p.sbuf("dnorm", [128, 4], F32)
    dotb = p.sbuf("dotb", [128, 2048], BF16)
    pa_sb = p.sbuf("pa_sb", [128, 256], BF16)
    pb_sb = p.sbuf("pb_sb", [16, D], BF16)
    tp_sb = p.sbuf("tp_sb", [16, 1024], BF16)
    rows4 = p.sbuf("rows4", [1, 3072], F32)
    ssum = rows4[0:1, 0:1024]
    ssq = rows4[0:1, 1024:2048]
    arow = rows4[0:1, 2048:3072]
    brow = rows4[0:1, 1024:2048]
    a_b = p.sbuf("a_b", [128, 1024], F32)
    b_b = p.sbuf("b_b", [128, 1024], F32)
    xtmp = p.sbuf("xtmp", [128, 1024], F32)[:]
    sqs = [p.sbuf(f"sqs{i}", [128, 512], BF16) for i in range(2)]
    eps12 = p.sbuf("eps12", [128, 1], F32)
    ones1 = p.sbuf("ones1", [128, 1], BF16)
    eps_b = p.sbuf("eps_b", [128, 1], F32)
    banks = [p.psum(f"bank{i}", [128, 512]) for i in range(8)]

    ldc = p.sem()
    mld = p.sem()
    spe = p.sem()
    smax = p.sem()
    sexp = p.sem()
    tpe = p.sem()
    tev = p.sem()
    ape = p.sem()
    avd = p.sem()
    epi = p.sem()
    xld = p.sem()
    tbpe = p.sem()
    tbev = p.sem()
    lrpe = p.sem()
    lrev = p.sem()
    wsem = p.sem()
    sto = p.sem()
    block = p.begin()

    @block.sync
    def _(sync):
        ldc.inc(sync.dma_start(out=idf[:], in_=ident_in))
        ldc.inc(sync.dma_start(out=pa_sb[:], in_=pa_in))
        ldc.inc(sync.dma_start(out=pb_sb[:], in_=pb_in))
        for c in range(16):
            ldc.inc(sync.dma_start(out=q_sb[:, c * 1024:(c + 1) * 1024],
                                   in_=q_in[c * 128:(c + 1) * 128, :]))
        for c in range(4):
            ldc.inc(sync.dma_start(out=K_sb[:, c * 2048:(c + 1) * 2048],
                                   in_=kT[c * 128:(c + 1) * 128, :]))
        for j in range(16):
            ldc.inc(sync.dma_start(out=vp_sb[:, j * 516:(j + 1) * 516],
                                   in_=v_pad[j * 128:(j + 1) * 128, :]))
        for j in range(8):
            ldc.inc(sync.dma_start(out=vo_sb[:, j * 516:(j + 1) * 516],
                                   in_=v_own[j * 128:(j + 1) * 128, :]))
        for qt in range(8):
            if qt >= 1:
                spe.wait(sync, 64 * qt)
            mld.inc(sync.dma_start(out=msk[:], in_=mask[qt * 128:(qt + 1) * 128, :]))

    @block.vector
    def _(v):
        v = W(v)
        v.memset(ones1[:], 1.0)
        v.memset(eps_b[:], LN_EPS)
        v.memset(eps12[:], 1e-12)
        ldc.wait(v, 16)
        v.tensor_copy(identb[:], idf[:])

    NLD = 16 * (3 + 16 + 4 + 16 + 8)

    @block.tensor
    def _(t):
        ldc.wait(t, NLD)
        for qt in range(8):
            for h in range(16):
                g = qt * 16 + h
                kv = h // 4
                mld.wait(t, 16 * (qt + 1))
                if g >= 1:
                    sexp.wait(t, 4 * g)
                    avd.wait(t, g)
                lhs = q_sb[:, h * 1024 + qt * 128: h * 1024 + (qt + 1) * 128]
                for bnk in range(4):
                    t.matmul(banks[bnk][:, 0:512], lhs,
                             K_sb[:, kv * 2048 + bnk * 512: kv * 2048 + (bnk + 1) * 512],
                             start=True, stop=False)
                    spe.inc(t.matmul(banks[bnk][:, 0:512], identb[:],
                                     msk[:, bnk * 512:(bnk + 1) * 512],
                                     start=False, stop=True), 1)
                if g >= 1:
                    tev.wait(t, 4 * g)
                sexp.wait(t, 4 * (g + 1))
                for j in range(16):
                    inst = t.matmul(banks[4 + j // 4][:, (j % 4) * 128:(j % 4) * 128 + 128],
                                    P[:, j * 128:(j + 1) * 128], identb[:],
                                    start=True, stop=True)
                    if j % 4 == 3:
                        tpe.inc(inst, 1)
                tev.wait(t, 4 * (g + 1))
                for j in range(16):
                    inst = t.matmul(banks[0][:, 0:129],
                                    PT[:, j * 128:(j + 1) * 128],
                                    vp_sb[:, j * 516 + kv * 129: j * 516 + (kv + 1) * 129],
                                    start=(j == 0), stop=(j == 15))
                ape.inc(inst, 1)

    @block.vector
    def _(v):
        v = W(v)
        for g in range(128):
            spe.wait(v, 4 * (g + 1))
            for bnk in range(4):
                v.tensor_reduce(nm4[:, bnk:bnk + 1], banks[bnk][:, 0:512],
                                axis=AX.X, op=ALU.max, negate=True)
                v.drain()
            v.tensor_reduce(nm1[:], nm4[:], axis=AX.X, op=ALU.min)
            v.drain()
            smax.inc(v.nop(), 1)
            ape.wait(v, g + 1)
            qt, h = divmod(g, 16)
            v.reciprocal(rec[:], banks[0][:, 128:129])
            v.drain()
            avd.inc(v.tensor_scalar(
                y_tm[:, qt * 2048 + h * 128: qt * 2048 + (h + 1) * 128],
                banks[0][:, 0:128], rec[:], None, op0=ALU.mult), 1)
            if h == 15:
                vo = vo_sb[:, qt * 516:(qt + 1) * 516]
                vo4 = vo.rearrange("p (k d) -> p k d", d=129)[:, :, 0:128]
                v.tensor_tensor(ztmp[:, 0:516], vo, vo, ALU.mult)
                v.drain()
                zt4 = ztmp[:, 0:516].rearrange("p (k d) -> p k d", d=129)[:, :, 0:128]
                v.tensor_reduce(dnorm[:], zt4, axis=AX.X, op=ALU.add)
                v.drain()
                smax.inc(v.nop(), 1)
                epi.wait(v, 3 * qt + 2)
                v.tensor_scalar(dnorm[:], dnorm[:], 1e-12, None, op0=ALU.max)
                v.reciprocal(dnorm[:], dnorm[:])
                v.drain()
                vn4 = vn_sb[:].rearrange("p (k d) -> p k d", d=128)
                v.tensor_tensor(vn4, vo4,
                                dnorm[:].unsqueeze(2).broadcast_to([128, 4, 128]),
                                ALU.mult)
                v.drain()
                ysl = y_tm[:, qt * 2048:(qt + 1) * 2048]
                y4 = ysl.rearrange("p (k gg d) -> p k gg d", k=4, d=128)
                vnr = vn4.unsqueeze(2).broadcast_to([128, 4, 4, 128])
                v.tensor_tensor(ztmp[:].rearrange("p (k gg d) -> p k gg d", k=4, d=128),
                                y4, vnr, ALU.mult)
                v.drain()
                v.tensor_reduce(dot[:], ztmp[:].rearrange("p (hh d) -> p hh d", d=128),
                                axis=AX.X, op=ALU.add)
                v.drain()
                v.tensor_copy(dotb[:].rearrange("p (hh d) -> p hh d", d=128),
                              dot[:].unsqueeze(2).broadcast_to([128, 16, 128]))
                v.drain()
                d4 = dotb[:].rearrange("p (k gg d) -> p k gg d", k=4, d=128)
                v.tensor_tensor(ztmp[:].rearrange("p (k gg d) -> p k gg d", k=4, d=128),
                                d4, vnr, ALU.mult)
                v.drain()
                epi.inc(v.tensor_tensor(ysl, ysl, ztmp[:], ALU.subtract), 1)

    @block.scalar
    def _(s):
        s = W(s)
        nsm = 0
        for g in range(128):
            qt, h = divmod(g, 16)
            nsm += 1
            smax.wait(s, nsm)
            if g >= 1:
                tpe.wait(s, 4 * g)
            for bnk in range(4):
                sexp.inc(s.activation(P[:, bnk * 512:(bnk + 1) * 512],
                                      banks[bnk][:, 0:512], AF.Exp,
                                      bias=nm1[:]), 1)
            tpe.wait(s, 4 * (g + 1))
            if g >= 1:
                ape.wait(s, g)
            for bg in range(4):
                tev.inc(s.copy(PT[:, bg * 512:(bg + 1) * 512],
                               banks[4 + bg][:, 0:512]), 1)
            if h == 15:
                nsm += 1
                smax.wait(s, nsm)
                epi.inc(s.sqrt(dnorm[:], dnorm[:]), 2)

    @block.sync
    def _(sync):
        epi.wait(sync, 24)
        for c in range(16):
            xld.inc(sync.dma_start(out=x_sb[:, c * 1024:(c + 1) * 1024],
                                   in_=xT[c * 128:(c + 1) * 128, :]))

    y_fm = q_sb

    @block.tensor
    def _(t):
        epi.wait(t, 24)
        for grp in range(32):
            if grp >= 2:
                tbev.wait(t, 4 * (grp - 1))
            for jj in range(4):
                j = grp * 4 + jj
                qt, fc = divmod(j, 16)
                inst = t.matmul(banks[4 + grp % 2][:, jj * 128:(jj + 1) * 128],
                                y_tm[:, qt * 2048 + fc * 128: qt * 2048 + (fc + 1) * 128],
                                identb[:], start=True, stop=True)
            tbpe.inc(inst, 1)

    @block.scalar
    def _(s):
        s = W(s)
        for grp in range(32):
            tbpe.wait(s, grp + 1)
            for jj in range(4):
                j = grp * 4 + jj
                qt, fc = divmod(j, 16)
                tbev.inc(s.copy(
                    y_fm[:, fc * 1024 + qt * 128: fc * 1024 + (qt + 1) * 128],
                    banks[4 + grp % 2][:, jj * 128:(jj + 1) * 128]), 1)

    @block.tensor
    def _(t):
        tbev.wait(t, 128)
        for tt in range(2):
            for dc in range(16):
                inst = t.matmul(banks[6][0:16, 0:512],
                                pa_sb[:, dc * 16:(dc + 1) * 16],
                                y_fm[:, dc * 1024 + tt * 512: dc * 1024 + (tt + 1) * 512],
                                start=(dc == 0), stop=(dc == 15))
            lrpe.inc(inst, 1)
            lrev.wait(t, tt + 1)

    @block.scalar
    def _(s):
        s = W(s)
        for tt in range(2):
            lrpe.wait(s, tt + 1)
            lrev.inc(s.copy(tp_sb[0:16, tt * 512:(tt + 1) * 512],
                            banks[6][0:16, 0:512]), 1)

    proj_bf = y_tm

    def rhs_fn(kc, tt):
        return y_fm[:, kc * 1024 + tt * 512: kc * 1024 + (tt + 1) * 512]

    def extra_pb(t, n, tt, ps):
        return t.matmul(ps[:, 0:512], pb_sb[0:16, n * 128:(n + 1) * 128],
                        tp_sb[0:16, tt * 512:(tt + 1) * 512],
                        start=False, stop=True)

    def evac_proj(e, g, n, tt, ps):
        return e.copy(proj_bf[:, n * 1024 + tt * 512: n * 1024 + (tt + 1) * 512],
                      ps[:, 0:512])

    pegrp, evs = emit_mm_stream(
        p, block, wproj_t, 16, 16, rhs_fn, evac_proj, [wb0, wb1], wsem,
        banks[0:4], after=[(lrev, 2)], evac_engine="scalar", extra_mm=extra_pb)

    dn = emit_postln(p, block, proj_bf[:, 0:16384], x_sb, ssum, ssq, arow, brow,
                     a_b, b_b, scr2, banks[4:6], ones1, xtmp, eps_b, sqs,
                     after=[(evs, 32), (xld, 256)])

    @block.gpsimd
    def _(g):
        dn.wait(g, 48)
        for c in range(16):
            sto.inc(g.dma_start(out=x3_o[c * 128:(c + 1) * 128, :],
                                in_=x_sb[:, c * 1024:(c + 1) * 1024]))
        sto.wait(g)

    p.done()
    return nc



def build_f1():
    p = Prog()
    nc = p.nc
    dt = nc.dram_tensor

    xT = dt("xT", [D, T], F32, kind="ExternalInput").ap()
    wg_t = dt("wg_t", [128, 128, 16, 128], BF16, kind="ExternalInput").ap()
    wu_t = dt("wu_t", [128, 128, 16, 128], BF16, kind="ExternalInput").ap()
    wd_kt = dt("wd_kt", [128, 128, 16, 128], BF16, kind="ExternalInput").ap()
    x4_o = dt("x4_o", [D, T], F32, kind="ExternalOutput").ap()
    scr2 = dt("scr2", [2, 1024], F32, kind="Internal").ap()
    inter_d = dt("inter_d", [MLP, T], BF16, kind="Internal").ap()

    x_sb = p.sbuf("x_sb", [128, 16384], F32)[:]
    xn = p.sbuf("xn", [128, 16384], BF16)[:]
    y_bf = p.sbuf("y_bf", [128, 16384], BF16)[:]
    wgb = [p.sbuf(f"wgb{i}", [128, 2048], BF16) for i in range(2)]
    wub = [p.sbuf(f"wub{i}", [128, 2048], BF16) for i in range(2)]
    ibuf = [p.sbuf(f"ibuf{i}", [128, 1024], BF16) for i in range(2)]
    isb = [p.sbuf(f"isb{i}", [128, 512], BF16) for i in range(2)]
    wdb = [p.sbuf(f"wdb{i}", [128, 1024], BF16) for i in range(2)]
    sq_f = p.sbuf("sq_f", [128, 512], F32)
    u1_f = p.sbuf("u1_f", [128, 512], F32)
    sg_f = p.sbuf("sg_f", [128, 512], F32)
    gg_f = p.sbuf("gg_f", [128, 512], F32)
    rows4 = p.sbuf("rows4", [1, 3072], F32)
    ssum = rows4[0:1, 0:1024]
    ssq = rows4[0:1, 1024:2048]
    arow = rows4[0:1, 2048:3072]
    brow = rows4[0:1, 1024:2048]
    a_b = p.sbuf("a_b", [128, 1024], F32)
    b_b = p.sbuf("b_b", [128, 1024], F32)
    xtmp = p.sbuf("xtmp", [128, 1024], F32)[:]
    sqs = [p.sbuf(f"sqs{i}", [128, 512], BF16) for i in range(2)]
    ones1 = p.sbuf("ones1", [128, 1], BF16)
    eps_b = p.sbuf("eps_b", [128, 1], F32)
    banks = [p.psum(f"bank{i}", [128, 512]) for i in range(8)]

    xb = y_bf[:, 0:16384]
    sq = xn

    ld = p.sem()
    gw = p.sem()
    gpe = p.sem()
    gev = p.sem()
    ist = p.sem()
    il = p.sem()
    dw = p.sem()
    dpe = p.sem()
    dev_ = p.sem()
    sto = p.sem()
    block = p.begin()

    @block.sync
    def _(sync):
        for c in range(16):
            ld.inc(sync.dma_start(out=x_sb[:, c * 1024:(c + 1) * 1024],
                                  in_=xT[c * 128:(c + 1) * 128, :]))

    @block.vector
    def _(v):
        v = W(v)
        v.memset(ones1[:], 1.0)
        v.memset(eps_b[:], LN_EPS)

    dn = emit_ln(p, block, x_sb, xn, xb, sq, ssum, ssq, arow, brow, a_b, b_b,
                 scr2, banks[0:2], ones1, xtmp, eps_b, after=[(ld, 256)])

    sA = p.sem()
    vA = p.sem()

    @block.sync
    def _(sync):
        dn.wait(sync, 32)
        for n in range(128):
            if n >= 2:
                gpe.wait(sync, 2 * (n - 1))
            gw.inc(sync.dma_start(
                out=wgb[n % 2][:].rearrange("p (k m) -> p k m", m=128),
                in_=wg_t[n]))
            gw.inc(sync.dma_start(
                out=wub[n % 2][:].rearrange("p (k m) -> p k m", m=128),
                in_=wu_t[n]))

    @block.tensor
    def _(t):
        dn.wait(t, 32)
        for n in range(128):
            gw.wait(t, 32 * (n + 1))
            for tt in range(2):
                g = 2 * n + tt
                bg = banks[2 + (g % 2)]
                bu = banks[4 + (g % 2)]
                if g >= 2:
                    gev.wait(t, g - 1)
                for kc in range(16):
                    rhs = xn[:, kc * 1024 + tt * 512: kc * 1024 + (tt + 1) * 512]
                    t.matmul(bg[:, 0:512], wgb[n % 2][:, kc * 128:(kc + 1) * 128],
                             rhs, start=(kc == 0), stop=(kc == 15))
                for kc in range(16):
                    rhs = xn[:, kc * 1024 + tt * 512: kc * 1024 + (tt + 1) * 512]
                    inst = t.matmul(bu[:, 0:512], wub[n % 2][:, kc * 128:(kc + 1) * 128],
                                    rhs, start=(kc == 0), stop=(kc == 15))
                gpe.inc(inst, 1)

    @block.scalar
    def _(s):
        s = W(s)
        for g in range(256):
            gpe.wait(s, g + 1)
            bg = banks[2 + (g % 2)]
            sA.inc(s.activation(sq_f[:], bg[:, 0:512], AF.Square), 1)
            vA.wait(s, g + 1)
            sA.inc(s.activation(sg_f[:], u1_f[:], AF.Sigmoid), 1)

    @block.vector
    def _(v):
        v = W(v)
        for g in range(256):
            n, tt = divmod(g, 2)
            bg = banks[2 + (g % 2)]
            bu = banks[4 + (g % 2)]
            sA.wait(v, 2 * g + 1)
            v.tensor_scalar(u1_f[:], sq_f[:], 0.07135482, 1.59576912,
                            op0=ALU.mult, op1=ALU.add)
            vA.inc(v.tensor_tensor(u1_f[:], u1_f[:], bg[:, 0:512], ALU.mult), 1)
            sA.wait(v, 2 * g + 2)
            v.tensor_tensor(gg_f[:], sg_f[:], bg[:, 0:512], ALU.mult)
            if n >= 2 and tt == 0:
                ist.wait(v, 16 * (n - 1))
            gev.inc(v.tensor_tensor(
                ibuf[n % 2][:, tt * 512:(tt + 1) * 512], gg_f[:],
                bu[:, 0:512], ALU.mult), 1)

    @block.gpsimd
    def _(g):
        for n in range(128):
            gev.wait(g, 2 * n + 2)
            ist.inc(g.dma_start(out=inter_d[n * 128:(n + 1) * 128, :],
                                in_=ibuf[n % 2][:]))

    @block.sync
    def _(sync):
        kci = 0
        for nh in range(2):
            for tt in range(2):
                for kc in range(128):
                    if kci == 0:
                        ist.wait(sync, 16 * 128)
                    if kci >= 2:
                        dpe.wait(sync, kci - 1)
                    il.inc(sync.dma_start(
                        out=isb[kci % 2][:],
                        in_=inter_d[kc * 128:(kc + 1) * 128,
                                    tt * 512:(tt + 1) * 512]))
                    il.inc(sync.dma_start(
                        out=wdb[kci % 2][:].rearrange("p (a m) -> p a m", m=128),
                        in_=wd_kt[kc][:, nh * 8:(nh + 1) * 8, :]))
                    kci += 1

    @block.tensor
    def _(t):
        kci = 0
        for nh in range(2):
            for tt in range(2):
                pss = nh * 2 + tt
                if pss >= 1:
                    dev_.wait(t, 8 * pss)
                for kc in range(128):
                    il.wait(t, 32 * (kci + 1))
                    for n8 in range(8):
                        inst = t.matmul(
                            banks[n8][:, 0:512],
                            wdb[kci % 2][:, n8 * 128:(n8 + 1) * 128],
                            isb[kci % 2][:],
                            start=(kc == 0), stop=(kc == 127))
                    dpe.inc(inst, 1)
                    kci += 1

    @block.scalar
    def _(s):
        s = W(s)
        kci = 0
        for nh in range(2):
            for tt in range(2):
                kci += 128
                dpe.wait(s, kci)
                for n8 in range(8):
                    nn = nh * 8 + n8
                    dev_.inc(s.copy(
                        y_bf[:, nn * 1024 + tt * 512: nn * 1024 + (tt + 1) * 512],
                        banks[n8][:, 0:512]), 1)

    dn2 = emit_postln(p, block, y_bf, x_sb, ssum, ssq, arow, brow,
                      a_b, b_b, scr2, banks[2:4], ones1, xtmp, eps_b, sqs,
                      after=[(dev_, 32)])

    @block.gpsimd
    def _(g):
        dn2.wait(g, 48)
        for c in range(16):
            sto.inc(g.dma_start(out=x4_o[c * 128:(c + 1) * 128, :],
                                in_=x_sb[:, c * 1024:(c + 1) * 1024]))
        sto.wait(g)

    p.done()
    return nc



def _rope_tables(pos):
    inv = 1.0 / (10000.0 ** (np.arange(0, 64, 2, dtype=np.float32) / 64.0))
    fr = pos[:, None].astype(np.float32) * inv[None, :]
    cos, sin = np.cos(fr).astype(np.float32), np.sin(fr).astype(np.float32)
    C = np.concatenate([cos.T, cos.T, np.ones((64, len(pos)), np.float32)], 0)
    S = np.concatenate([sin.T, -sin.T, np.zeros((64, len(pos)), np.float32)], 0)
    return np.ascontiguousarray(np.concatenate([C, S], 0))


def _amajor(a):
    return np.ascontiguousarray(a.reshape(16, 128, 16).transpose(1, 0, 2).reshape(128, 256))


def kernel(**inputs):
    inp = {k: np.asarray(v) for k, v in inputs.items()}
    cores = list(range(8))

    maps, shapes, n_cast_rows = _prep_host_inputs(inp)
    res = run_bass_kernel_spmd(build_preprocess(n_cast_rows), maps, core_ids=cores)
    prep = _prep_host_outputs(res.results, shapes)

    x = inp["L0_attn_x"]
    ident = np.eye(128, dtype=np.float32)

    def xTs(c, xfull):
        b, h = c // 2, c % 2
        return np.ascontiguousarray(xfull[b].T[PI][:, h * 1024:(h + 1) * 1024])

    wqkv_b = prep["L0_attn_w_qkv"]
    wqk_t = tile_lhsT(np.ascontiguousarray(wqkv_b[:, 0:4096]))
    wv_r = np.ascontiguousarray(
        wqkv_b[:, 4096:6144].reshape(16, 128, 8, 256).transpose(2, 1, 0, 3))
    maps = [dict(xT=xTs(c, x), wqk_t=wqk_t, wv_r=wv_r) for c in cores]
    resa = run_bass_kernel_spmd(build_a0a(), maps, core_ids=cores)

    wout_t = tile_lhsT(np.ascontiguousarray(prep["L0_attn_w_out"]))
    maps = []
    for c in cores:
        b, h = c // 2, c % 2
        kfull = np.concatenate([resa.results[2 * b]["k_o"],
                                resa.results[2 * b + 1]["k_o"]], axis=1)
        vfull = np.concatenate([resa.results[2 * b]["v_o"],
                                resa.results[2 * b + 1]["v_o"]], axis=0)
        v_pad = np.concatenate([vfull.reshape(S, HEADS, 128),
                                np.ones((S, HEADS, 1), BF)], axis=2).reshape(S, HEADS * 129)
        qpos = np.arange(h * T, (h + 1) * T)[:, None]
        mask = np.where(qpos >= np.arange(S)[None, :], 0.0, NEG).astype(BF)
        maps.append(dict(q_in=resa.results[c]["q_o"], kT=np.ascontiguousarray(kfull),
                         v_pad=np.ascontiguousarray(v_pad), mask=mask,
                         xT=xTs(c, x), wout_t=wout_t, ident_in=ident))
    resb = run_bass_kernel_spmd(build_a0b(), maps, core_ids=cores)
    x1 = [resb.results[c]["x1_o"] for c in cores]

    wfc_t = tile_lhsT(np.ascontiguousarray(prep["wfct"]))
    wpj_t = tile_lhsT(np.ascontiguousarray(prep["wpjt"]))
    maps = [dict(xT=x1[c], wfc_t=wfc_t, wpj_t=wpj_t) for c in cores]
    resf = run_bass_kernel_spmd(build_f0(), maps, core_ids=cores)
    x2 = [resf.results[c]["x2_o"] for c in cores]

    wq_t = tile_lhsT(np.ascontiguousarray(prep["L1_attn_Wq"]))
    wk_t = tile_lhsT(np.ascontiguousarray(prep["L1_attn_Wk"]))
    wv1_r = np.ascontiguousarray(
        prep["L1_attn_Wv"].reshape(16, 128, 2, 256).transpose(2, 1, 0, 3))
    qa_in = _amajor(prep["L1_attn_q_A"])
    ka_in = _amajor(prep["L1_attn_k_A"])
    va_in = _amajor(prep["L1_attn_v_A"])
    maps = []
    for c in cores:
        b, h = c // 2, c % 2
        pos = np.arange(h * T, (h + 1) * T)
        maps.append(dict(xT=x2[c], wq_t=wq_t, wk_t=wk_t, wv_r=wv1_r, qa_in=qa_in,
                         ka_in=ka_in, va_in=va_in, qb_in=prep["L1_attn_q_B"],
                         kb_in=prep["L1_attn_k_B"], vb_in=prep["L1_attn_v_B"],
                         gain_in=np.ascontiguousarray(
                             inp["L1_attn_q_gain"].reshape(1, 16).astype(np.float32)),
                         cs_in=_rope_tables(pos)))
    resq = run_bass_kernel_spmd(build_a1a(), maps, core_ids=cores)

    wproj_t = tile_lhsT(np.ascontiguousarray(prep["L1_attn_Wproj"]))
    pa_in = _amajor(prep["L1_attn_p_A"])
    maps = []
    for c in cores:
        b, h = c // 2, c % 2
        kfull = np.concatenate([resq.results[2 * b]["k_o"],
                                resq.results[2 * b + 1]["k_o"]], axis=1)
        vfull = np.concatenate([resq.results[2 * b]["v_o"],
                                resq.results[2 * b + 1]["v_o"]], axis=0)
        v_pad = np.concatenate([vfull.reshape(S, HKV, 128),
                                np.ones((S, HKV, 1), BF)], axis=2).reshape(S, HKV * 129)
        qpos = np.arange(h * T, (h + 1) * T)[:, None]
        mask = np.where(qpos >= np.arange(S)[None, :], 0.0, NEG).astype(BF)
        maps.append(dict(q_in=resq.results[c]["q_o"], kT=np.ascontiguousarray(kfull),
                         v_pad=np.ascontiguousarray(v_pad),
                         v_own=np.ascontiguousarray(v_pad[h * T:(h + 1) * T]),
                         mask=mask, xT=x2[c], wproj_t=wproj_t, pa_in=pa_in,
                         pb_in=prep["L1_attn_p_B"], ident_in=ident))
    resb1 = run_bass_kernel_spmd(build_a1b(), maps, core_ids=cores)
    x3 = [resb1.results[c]["x3_o"] for c in cores]

    wg_t = tile_lhsT(np.ascontiguousarray(prep["Wg_t"]))
    wu_t = tile_lhsT(np.ascontiguousarray(prep["Wu_t"]))
    wd_kt = np.ascontiguousarray(prep["Wd_t"].reshape(128, 128, 16, 128))
    maps = [dict(xT=x3[c], wg_t=wg_t, wu_t=wu_t, wd_kt=wd_kt) for c in cores]
    resf1 = run_bass_kernel_spmd(build_f1(), maps, core_ids=cores)

    out = np.empty((4, S, D), np.float32)
    for c in cores:
        b, h = c // 2, c % 2
        xp = resf1.results[c]["x4_o"]
        nat = np.empty_like(xp)
        nat[PI, :] = xp
        out[b, h * T:(h + 1) * T, :] = nat.T
    return out
